# version 32
# baseline (speedup 1.0000x reference)
import contextlib
import numpy as np
import concourse.bass as bass
import concourse.mybir as mybir
from concourse.bass_utils import run_bass_kernel_spmd

F32 = mybir.dt.float32
BF16 = mybir.dt.bfloat16
I8 = mybir.dt.int8
AF = mybir.ActivationFunctionType
OP = mybir.AluOpType

NCORES = 8
D = 4096
A = 32
NCH = 2
N = 512
HB = 8
SI = 86
O2T = 32
QB = [0, 15, 30, 44, 58, 72, 86]
NQV = 6
QMAX = 15
DEFER = True
DRAIN_ALL_AT_0 = False
LN_EPS = 1e-5
ATT_EPS = 1e-6


def build_nc():
    nc = bass.Bass("TRN2", target_bir_lowering=False)

    xt = nc.dram_tensor("xt", [NCH, 128, A * N], F32, kind="ExternalInput")
    wqd = nc.dram_tensor("wqd", [HB, 128, A * 128], F32, kind="ExternalInput")
    bqd = nc.dram_tensor("bqd", [128, HB], F32, kind="ExternalInput")
    wod = nc.dram_tensor("wod", [A, 128, A * 128], F32, kind="ExternalInput")
    mwd = nc.dram_tensor("mwd", [HB, 128, 128], F32, kind="ExternalInput")
    ztd = nc.dram_tensor("ztd", [128, HB], F32, kind="ExternalInput")
    wgd = nc.dram_tensor("wgd", [SI, 128, A * 128], I8, kind="ExternalInput")
    wud = nc.dram_tensor("wud", [SI, 128, A * 128], I8, kind="ExternalInput")
    wdd = nc.dram_tensor("wdd", [O2T, 128, SI * 128], I8, kind="ExternalInput")
    sgd = nc.dram_tensor("sgd", [128, SI], F32, kind="ExternalInput")
    sud = nc.dram_tensor("sud", [128, SI], F32, kind="ExternalInput")
    sdd = nc.dram_tensor("sdd", [128, O2T], F32, kind="ExternalInput")
    idnd = nc.dram_tensor("idnd", [128, 128], F32, kind="ExternalInput")
    outd = nc.dram_tensor("outd", [NCH, 128, A * N], F32, kind="ExternalOutput")
    woed = nc.dram_tensor("woed", [A, 128, HB * 128], BF16)

    es = contextlib.ExitStack()
    sbt = lambda name, shape, dt: es.enter_context(nc.sbuf_tensor(name, shape, dt))
    pst = lambda name: es.enter_context(nc.psum_tensor(name, [128, 512], F32))

    r = sbt("r", [128, A * N], F32)
    xbuf = sbt("xbuf", [128, A * N], BF16)
    hq = sbt("hq", [128, QMAX * N], BF16)
    sigq = [sbt(f"sigq{i}", [128, N], BF16) for i in range(2)]
    retr = sbt("retr", [128, HB * N], BF16)
    wq_s = [sbt(f"wq{i}", [128, 16 * 128], BF16) for i in range(3)]
    woe_s = [sbt(f"woe{i}", [128, HB * 128], BF16) for i in range(3)]
    wg_s = [sbt(f"wgs{i}", [128, 16 * 128], BF16) for i in range(3)]
    wu_s = [sbt(f"wus{i}", [128, 16 * 128], BF16) for i in range(3)]
    wd_s = [sbt(f"wds{i}", [128, QMAX * 128], BF16) for i in range(3)]
    mz = sbt("mz", [128, HB * 128], BF16)
    zrep = sbt("zrep", [128, HB * 128], BF16)
    zb = sbt("zb", [128, HB], BF16)
    bqc = sbt("bqc", [128, HB], F32)
    nbq = sbt("nbq", [128, HB], F32)
    sgc = sbt("sgc", [128, SI], F32)
    suc = sbt("suc", [128, SI], F32)
    sdc = sbt("sdc", [128, O2T], F32)
    scr1 = sbt("scr1", [128, N], F32)
    rden = sbt("rden", [128, N], F32)
    m_sb = scr1[0:1, 0:N]
    hsig = [sbt("hsig0", [128, N], BF16)]
    xsb = [sbt(f"xsb{i}", [128, N], BF16) for i in range(2)]
    sigr = xsb
    sqb = sigq
    ones_row = sbt("ones_row", [1, 128], F32)
    invcol = sbt("invcol", [128, 1], BF16)
    e2_sb = sbt("e2_sb", [1, N], F32)
    isr = sbt("isr", [1, N], F32)
    slb = [sbt(f"slb{i}", [128, 4 * 512], BF16) for i in range(3)]
    idnb = sbt("idnb", [128, 128], BF16)
    woebld = [sbt(f"woebld{i}", [128, HB * 128], BF16) for i in range(2)]
    pb = [pst(f"pb{i}") for i in range(8)]

    semnames = (["pe", "ve", "se", "c0", "c1", "oo"]
                + [f"xr{k}" for k in range(4)] + [f"xb{k}" for k in range(4)]
                + [f"sl{k}" for k in range(3)] + [f"wq{k}" for k in range(3)]
                + [f"wg{k}" for k in range(3)] + [f"wu{k}" for k in range(3)]
                + [f"wd{k}" for k in range(3)] + [f"wo{k}" for k in range(3)]
                + [f"ws{k}" for k in range(2)])
    sems = {k: es.enter_context(nc.semaphore(k)) for k in semnames}

    E = {k: [] for k in ("t", "v", "s", "y", "g")}
    WC = {k: {} for k in E}
    C = {k: 0 for k in semnames}

    def w(e, ctr, v):
        if v is None or v <= 0:
            return
        if WC[e].get(ctr, 0) >= v:
            return
        WC[e][ctr] = v
        E[e].append(lambda eng, s=sems[ctr], vv=v: eng.wait_ge(s, vv))

    def op(e, fn, ctr=None, n=1):
        if ctr is None:
            E[e].append(fn)
            return None
        C[ctr] += n
        E[e].append(lambda eng, f=fn, s=sems[ctr], m=n: f(eng).then_inc(s, m))
        return C[ctr]

    def dma(e, dst, src, key):
        return op(e, lambda eng, d=dst, s=src: eng.dma_start(out=d, in_=s), key, 16)

    def mm(bank_ap, lhs_ap, rhs_ap, start, stop, ctr=None):
        fn = lambda eng, o=bank_ap, l=lhs_ap, rr=rhs_ap, st=start, sp=stop: \
            eng.matmul(o, l, rr, start=st, stop=sp)
        return op("t", fn, ctr)

    def act(out_ap, in_ap, func, bias=0.0, scale=1.0, ctr=None):
        fn = lambda eng, o=out_ap, i=in_ap, f=func, b=bias, sc=scale: \
            eng.activation(o, i, f, bias=b, scale=sc)
        return op("s", fn, ctr)

    BG = [("pe", 0)] * 8

    ST = {
        "wq": [0, 0, 0], "wg": [0, 0, 0], "wu": [0, 0, 0], "wd": [0, 0, 0],
        "woe": [0, 0, 0],
        "sigq": [0, 0],
        "retr": 0,
        "hq": 0,
        "hsig": [0],
        "xs": [0, 0],
        "retr_ve": 0,
        "slb": [0, 0, 0],
    }
    dg_xb = []
    ds_x = []

    def xb_piece(c, k):
        return dma("g", xbuf[:, k * 8 * N:(k + 1) * 8 * N],
                   bass.AP(xt, c * 128 * A * N + k * 8 * N,
                           [[A * N, 128], [1, 8 * N]]), f"xb{k}")

    def slb_dma(h2):
        s, q_ = divmod(h2, 2)
        slot = h2 % 3
        w("g", "pe", ST["slb"][slot])
        return dma("g", slb[slot][:, :],
                   bass.AP(wod, s * 128 * A * 128 + q_ * 2048,
                           [[A * 128, 128], [1, 2048]]), f"sl{slot}")

    dma("y", bqc[:, :], bqd.ap()[:, :], "c0")
    dma("y", sgc[:, :], sgd.ap()[:, :], "c0")
    dma("y", suc[:, :], sud.ap()[:, :], "c0")
    ds_consts = dma("y", sdc[:, :], sdd.ap()[:, :], "c0")
    ds_x.append([dma("y", r[:, k * 8 * N:(k + 1) * 8 * N],
                     bass.AP(xt, k * 8 * N, [[A * N, 128], [1, 8 * N]]), f"xr{k}")
                 for k in range(4)])

    dma("g", bass.AP(mz, 0, [[HB * 128, 128], [128, HB], [1, 128]]),
        bass.AP(mwd, 0, [[128, 128], [128 * 128, HB], [1, 128]]), "c1")
    dma("g", zb[:, :], ztd.ap()[:, :], "c1")
    dg_mz = dma("g", idnb[:, :], idnd.ap()[:, :], "c1")
    dg_xb.append([xb_piece(0, k) for k in range(4)])

    op("v", lambda e: e.memset(ones_row[:, :], 1.0))
    op("v", lambda e: e.memset(invcol[:, :], 1.0 / 4096.0))
    w("v", "c0", ds_consts)
    op("v", lambda e: e.tensor_scalar_mul(nbq[:, :], bqc[:, :], -1.0))
    w("v", "c1", dg_mz)
    ve_consts = None
    for g_ in range(HB):
        src = bass.AP(zb, g_, [[HB, 128], [0, 128]])
        ve_consts = op("v", lambda e, d=zrep[:, g_ * 128:(g_ + 1) * 128], s=src:
                       e.tensor_copy(d, s), "ve" if g_ == HB - 1 else None)

    def emit_ln(r_ready_ve, is_ln1, defer_apply=False):
        pe_stat = [0] * A
        for a in range(A):
            w("s", "ve", r_ready_ve[a])
            w("s", "pe", ST["xs"][a % 2])
            ra = r[:, a * N:(a + 1) * N]
            se_c1 = act(xsb[a % 2][:, :], ra, AF.Copy, ctr="se")
            w("v", "ve", r_ready_ve[a])
            w("v", "pe", ST["xs"][a % 2])
            ve_c1 = op("v", lambda e, o=sqb[a % 2][:, :], i=ra:
                       e.tensor_mul(o, i, i), "ve")
            if a == 0:
                w("t", *BG[6])
                w("t", *BG[7])
                w("t", "ve", ve_consts)
            w("t", "se", se_c1)
            w("t", "ve", ve_c1)
            mm(pb[6][0:1, 0:N], invcol[:, :], xsb[a % 2][:, :], a == 0, a == A - 1)
            pe_stat[a] = mm(pb[7][0:1, 0:N], invcol[:, :], sqb[a % 2][:, :],
                            a == 0, a == A - 1, ctr="pe")
            ST["xs"][a % 2] = pe_stat[a]
        w("s", "pe", pe_stat[A - 1])
        act(m_sb, pb[6][0:1, 0:N], AF.Copy)
        se_rows1 = act(e2_sb[:, :], pb[7][0:1, 0:N], AF.Copy, ctr="se")
        BG[6] = BG[7] = ("se", se_rows1)
        w("v", "se", se_rows1)
        vm = op("v", lambda e: e.tensor_mul(isr[:, :], m_sb, m_sb), "ve")
        w("v", "ve", vm)
        ve_rows = op("v", lambda e: e.scalar_tensor_tensor(
            e2_sb[:, :], e2_sb[:, :], LN_EPS, isr[:, :],
            op0=OP.add, op1=OP.subtract), "ve")
        w("s", "ve", ve_rows)
        seL = act(e2_sb[:, :], e2_sb[:, :], AF.Ln, ctr="se")
        w("s", "se", seL)
        se_rows2 = act(isr[:, :], e2_sb[:, :], AF.Exp, scale=-0.5, ctr="se")
        w("t", *BG[6])
        w("t", *BG[7])
        w("t", "se", se_rows2)
        mm(pb[6][:, 0:N], ones_row[:, :], m_sb, True, True)
        pe_bc = mm(pb[7][:, 0:N], ones_row[:, :], isr[:, :], True, True, ctr="pe")
        if defer_apply:
            return None, None, pe_bc
        ve_apply = [0] * A
        se_cast2 = [0] * A if is_ln1 else None
        w("v", "pe", pe_bc)
        ve_sub = [0] * A

        def esub(a):
            ra = r[:, a * N:(a + 1) * N]
            ve_sub[a] = op("v", lambda e, o=ra, i=pb[6][:, 0:N]:
                           e.tensor_sub(o, o, i), "ve")

        esub(0)
        for a in range(A):
            if a + 1 < A:
                esub(a + 1)
            ra = r[:, a * N:(a + 1) * N]
            w("v", "ve", ve_sub[a])
            ve_apply[a] = op("v", lambda e, o=ra, i=pb[7][:, 0:N]:
                             e.tensor_mul(o, o, i), "ve")
            if is_ln1:
                w("s", "ve", ve_apply[a])
                se_cast2[a] = act(xbuf[:, a * N:(a + 1) * N], ra, AF.Copy, ctr="se")
        BG[6] = BG[7] = ("ve", ve_apply[A - 1])
        return ve_apply, se_cast2, None

    def wq_piece(k):
        hb, half = divmod(k, 2)
        slot = k % 3
        w("g", "pe", ST["wq"][slot])
        return dma("g", wq_s[slot][:, :],
                   bass.AP(wqd, hb * 128 * A * 128 + half * 2048,
                           [[A * 128, 128], [1, 2048]]), f"wq{slot}")

    def wgu_piece(which, k):
        s, half = divmod(k, 2)
        slot = k % 3
        bufs, dr, st_key = ((wg_s, wgd, "wg") if which == "g" else (wu_s, wud, "wu"))
        w("g", "pe", ST[st_key][slot])
        return dma("g", bufs[slot][:, :],
                   bass.AP(dr, s * 128 * A * 128 + half * 2048,
                           [[A * 128, 128], [1, 2048]]), f"{st_key}{slot}")

    wd_slot_of = {}
    wd_seq = [0]

    def wd_piece(k):
        o2, qv = divmod(k, NQV)
        nk = QB[qv + 1] - QB[qv]
        slot = wd_seq[0] % 3
        wd_seq[0] += 1
        wd_slot_of[k] = slot
        w("g", "pe", ST["wd"][slot])
        return dma("g", wd_s[slot][:, 0:nk * 128],
                   bass.AP(wdd, o2 * 128 * SI * 128 + QB[qv] * 128,
                           [[SI * 128, 128], [1, nk * 128]]), f"wd{slot}")

    pend = None

    def drain_pend(upto):
        nonlocal_ = pend
        if nonlocal_ is None:
            return
        cp = nonlocal_["c"]
        w("v", "pe", nonlocal_["pe_bc"])
        last_ve = 0
        vsub = {}

        def esub2(a):
            ra = r[:, a * N:(a + 1) * N]
            vsub[a] = op("v", lambda e, o=ra, i=pb[6][:, 0:N]:
                         e.tensor_sub(o, o, i), "ve")

        lo = nonlocal_["a"]
        esub2(lo)
        for a in range(lo, upto):
            if a + 1 < upto:
                esub2(a + 1)
            ra = r[:, a * N:(a + 1) * N]
            w("v", "ve", vsub[a])
            last_ve = op("v", lambda e, o=ra, i=pb[7][:, 0:N]:
                         e.tensor_mul(o, o, i), "ve")
            if a % 8 == 7:
                k = a // 8
                w("y", "ve", last_ve)
                vo = dma("y", bass.AP(outd, cp * 128 * A * N + k * 8 * N,
                                      [[A * N, 128], [1, 8 * N]]),
                         r[:, k * 8 * N:(k + 1) * 8 * N], "oo")
                w("y", "oo", vo)
                nonlocal_["nxt"].append(
                    dma("y", r[:, k * 8 * N:(k + 1) * 8 * N],
                        bass.AP(xt, (cp + 1) * 128 * A * N + k * 8 * N,
                                [[A * N, 128], [1, 8 * N]]), f"xr{k}"))
        nonlocal_["a"] = upto
        if upto == A:
            BG[6] = BG[7] = ("ve", last_ve)
            ds_x.append(nonlocal_["nxt"])

    for c in range(NCH):
        dg_wq = [0] * (2 * HB)
        for k in range(3):
            dg_wq[k] = wq_piece(k)
        pe_q = [0] * HB
        pe_nd = [0] * HB
        ve_retr = [0] * HB

        def emit_q(hb):
            qb_ = pb[hb % 2]
            w("t", *BG[hb % 2])
            for a in range(A):
                if a % 8 == 0:
                    w("t", f"xb{a // 8}", dg_xb[c][a // 8])
                if a == 0:
                    w("t", f"wq{(hb * 2) % 3}", dg_wq[hb * 2])
                if a == 16:
                    w("t", f"wq{(hb * 2 + 1) % 3}", dg_wq[hb * 2 + 1])
                piece = wq_s[(hb * 2 + a // 16) % 3]
                pe_q[hb] = mm(qb_[:, 0:N], piece[:, (a % 16) * 128:(a % 16 + 1) * 128],
                              xbuf[:, a * N:(a + 1) * N], a == 0, a == A - 1,
                              ctr="pe" if a == A - 1 else None)
            ST["wq"][(hb * 2) % 3] = pe_q[hb]
            ST["wq"][(hb * 2 + 1) % 3] = pe_q[hb]
            for k in (hb * 2 + 3, hb * 2 + 4):
                if k < 2 * HB and k >= 3:
                    dg_wq[k] = wq_piece(k)

        prev_se2 = 0
        emit_q(0)
        for hb in range(HB):
            qb_ = pb[hb % 2]
            w("s", "pe", pe_q[hb])
            w("s", "ve", ve_consts)
            w("s", "pe", ST["sigq"][hb % 2])
            w("s", "se", prev_se2)
            se1 = act(scr1[:, :], qb_[:, 0:N], AF.Relu, bias=nbq[:, hb:hb + 1],
                      scale=-1.0, ctr="se")
            w("s", "se", se1)
            se2 = act(sigq[hb % 2][:, :], scr1[:, :], AF.Exp, scale=-1.0, ctr="se")
            prev_se2 = se2
            se_elu = act(sigr[hb % 2][:, :], qb_[:, 0:N], AF.Relu,
                         bias=bqc[:, hb:hb + 1], ctr="se")
            BG[hb % 2] = ("se", se_elu)
            if pend is not None:
                drain_pend(A if DRAIN_ALL_AT_0 else 4 * (hb + 1))
                if hb == HB - 1 or DRAIN_ALL_AT_0:
                    pend = None
            if hb + 1 < HB:
                emit_q(hb + 1)
            nb_, db_ = pb[2 + hb % 2], pb[4 + hb % 2]
            w("t", *BG[2 + hb % 2])
            w("t", *BG[4 + hb % 2])
            w("t", "se", se_elu)
            w("t", "c1", dg_mz)
            mm(nb_[:, 0:N], mz[:, hb * 128:(hb + 1) * 128], sigq[hb % 2][:, :],
               True, False)
            mm(nb_[:, 0:N], mz[:, hb * 128:(hb + 1) * 128], sigr[hb % 2][:, :],
               False, True)
            mm(db_[:, 0:N], zrep[:, hb * 128:(hb + 1) * 128], sigq[hb % 2][:, :],
               True, False)
            pe_nd[hb] = mm(db_[:, 0:N], zrep[:, hb * 128:(hb + 1) * 128],
                           sigr[hb % 2][:, :], False, True, ctr="pe")
            ST["sigq"][hb % 2] = pe_nd[hb]
            w("v", "pe", pe_nd[hb])
            w("v", "pe", ST["retr"])
            w("v", "ve", ve_retr[hb - 1] if hb > 0 else ST["retr_ve"])
            v1 = op("v", lambda e, o=rden[:, :], i=db_[:, 0:N]:
                    e.tensor_scalar_add(o, i, ATT_EPS), "ve")
            w("v", "ve", v1)
            v2 = op("v", lambda e, o=rden[:, :]: e.reciprocal(o, o), "ve")
            w("v", "ve", v2)
            ve_retr[hb] = op("v", lambda e, o=retr[:, hb * N:(hb + 1) * N],
                             i0=nb_[:, 0:N], i1=rden[:, :]: e.tensor_mul(o, i0, i1), "ve")
            BG[2 + hb % 2] = BG[4 + hb % 2] = ("ve", ve_retr[hb])
        ST["retr_ve"] = ve_retr[HB - 1]

        dg_wg = [0] * (2 * SI)
        dg_wu = [0] * (2 * SI)
        for k in range(3):
            dg_wg[k] = wgu_piece("g", k)
            dg_wu[k] = wgu_piece("u", k)

        pe_ao = [0] * A
        ve_aoev = [0] * A
        if c == 0:
            dg_slb = [0] * (2 * A)
            for k in range(3):
                w("g", "pe", pe_q[min(2 * k + 1, HB - 1)])
                dg_slb[k] = slb_dma(k)
            bld_ve = [0] * A
            ds_store = [0] * A
            for s in range(A):
                ve3 = 0
                for q_ in (0, 1):
                    h2 = s * 2 + q_
                    bank = pb[6 + (h2 % 2)]
                    w("t", *BG[6 + (h2 % 2)])
                    w("t", f"sl{h2 % 3}", dg_slb[h2])
                    w("t", "c1", dg_mz)
                    pe_bld = 0
                    for r_ in range(4):
                        rhs = bass.AP(slb[h2 % 3], r_ * 128,
                                      [[2048, 128], [512, 4], [1, 128]])
                        pe_bld = mm(bank[:, 0:512], idnb[:, :], rhs,
                                    r_ == 0, r_ == 3, ctr="pe" if r_ == 3 else None)
                    ST["slb"][h2 % 3] = pe_bld
                    if h2 + 3 < 2 * A:
                        dg_slb[h2 + 3] = slb_dma(h2 + 3)
                    w("v", "pe", pe_bld)
                    if s >= 2 and q_ == 0:
                        w("v", "pe", pe_ao[s - 2])
                        w("v", f"ws{s % 2}", ds_store[s - 2])
                    ve3 = op("v", lambda e,
                             o=woebld[s % 2][:, q_ * 512:(q_ + 1) * 512],
                             i=bank[:, 0:512]: e.tensor_copy(o, i), "ve")
                    BG[6 + (h2 % 2)] = ("ve", ve3)
                bld_ve[s] = ve3
                w("y", "ve", bld_ve[s])
                ds_store[s] = dma("y", bass.AP(woed, s * 128 * HB * 128,
                                               [[HB * 128, 128], [1, HB * 128]]),
                                  woebld[s % 2][:, :], f"ws{s % 2}")
                ab = pb[s % 2]
                w("t", *BG[s % 2])
                w("t", "ve", ve_retr[HB - 1])
                w("t", "ve", bld_ve[s])
                for g_ in range(HB):
                    pe_ao[s] = mm(ab[:, 0:N], woebld[s % 2][:, g_ * 128:(g_ + 1) * 128],
                                  retr[:, g_ * N:(g_ + 1) * N], g_ == 0, g_ == HB - 1,
                                  ctr="pe" if g_ == HB - 1 else None)
                w("v", "pe", pe_ao[s])
                w("v", f"xr{s // 8}", ds_x[c][s // 8])
                ve_aoev[s] = op("v", lambda e, o=r[:, s * N:(s + 1) * N], i=ab[:, 0:N]:
                                e.tensor_add(o, o, i), "ve")
                BG[s % 2] = ("ve", ve_aoev[s])
        else:
            ds_woe = [0] * A
            for s in range(3):
                ds_woe[s] = ST["woe_pre"][s]
            for s in range(A):
                ab = pb[s % 2]
                w("t", *BG[s % 2])
                w("t", "ve", ve_retr[HB - 1])
                w("t", f"wo{s % 3}", ds_woe[s])
                for g_ in range(HB):
                    pe_ao[s] = mm(ab[:, 0:N], woe_s[s % 3][:, g_ * 128:(g_ + 1) * 128],
                                  retr[:, g_ * N:(g_ + 1) * N], g_ == 0, g_ == HB - 1,
                                  ctr="pe" if g_ == HB - 1 else None)
                w("v", "pe", pe_ao[s])
                w("v", f"xr{s // 8}", ds_x[c][s // 8])
                ve_aoev[s] = op("v", lambda e, o=r[:, s * N:(s + 1) * N], i=ab[:, 0:N]:
                                e.tensor_add(o, o, i), "ve")
                BG[s % 2] = ("ve", ve_aoev[s])
                ST["woe"][s % 3] = pe_ao[s]
                if s + 3 < A:
                    w("g", "pe", pe_ao[s])
                    w("g", f"ws{(s + 3) % 2}", ST["ws_total"][(s + 3) % 2])
                    ds_woe[s + 3] = dma("g", woe_s[s % 3][:, :],
                                        bass.AP(woed, (s + 3) * 128 * HB * 128,
                                                [[HB * 128, 128], [1, HB * 128]]), f"wo{s % 3}")
        if c == 0:
            ST["ws_store"] = list(ds_store)
            ST["ws_total"] = {0: C["ws0"], 1: C["ws1"]}
        ST["retr"] = pe_ao[A - 1]

        ve_apply1, se_cast2, _ = emit_ln(ve_aoev, is_ln1=True)

        pe_gate = [0] * SI
        pe_up = [0] * SI
        dg_wd = [0] * (NQV * O2T)
        pe_down_last = 0
        ve_dev = [0] * A
        for qv in range(NQV):
            s0, s1 = QB[qv], QB[qv + 1]
            ve_hmul = [0] * (s1 - s0)
            for s in range(s0, s1):
                j = s - s0
                gb, ub = pb[s % 2], pb[2 + s % 2]
                w("t", *BG[s % 2])
                for a in range(A):
                    if a == 0:
                        w("t", f"wg{(s * 2) % 3}", dg_wg[s * 2])
                    if a == 16:
                        w("t", f"wg{(s * 2 + 1) % 3}", dg_wg[s * 2 + 1])
                    if qv == 0 and s == 0:
                        w("t", "se", se_cast2[a])
                    piece = wg_s[(s * 2 + a // 16) % 3]
                    pe_gate[s] = mm(gb[:, 0:N], piece[:, (a % 16) * 128:(a % 16 + 1) * 128],
                                    xbuf[:, a * N:(a + 1) * N], a == 0, a == A - 1,
                                    ctr="pe" if a == A - 1 else None)
                ST["wg"][(s * 2) % 3] = pe_gate[s]
                ST["wg"][(s * 2 + 1) % 3] = pe_gate[s]
                w("t", *BG[2 + s % 2])
                for a in range(A):
                    if a == 0:
                        w("t", f"wu{(s * 2) % 3}", dg_wu[s * 2])
                    if a == 16:
                        w("t", f"wu{(s * 2 + 1) % 3}", dg_wu[s * 2 + 1])
                    piece = wu_s[(s * 2 + a // 16) % 3]
                    pe_up[s] = mm(ub[:, 0:N], piece[:, (a % 16) * 128:(a % 16 + 1) * 128],
                                  xbuf[:, a * N:(a + 1) * N], a == 0, a == A - 1,
                                  ctr="pe" if a == A - 1 else None)
                ST["wu"][(s * 2) % 3] = pe_up[s]
                ST["wu"][(s * 2 + 1) % 3] = pe_up[s]
                for k in (s * 2 + 3, s * 2 + 4):
                    if 3 <= k < 2 * SI:
                        dg_wg[k] = wgu_piece("g", k)
                        dg_wu[k] = wgu_piece("u", k)
                if s == s1 - 2:
                    for kk in (qv, NQV + qv, 2 * NQV + qv):
                        if dg_wd[kk] == 0:
                            dg_wd[kk] = wd_piece(kk)
                w("s", "pe", pe_gate[s])
                w("s", "ve", ST["hsig"][0])
                w("s", "c0", ds_consts)
                se_silu = act(hsig[0][:, :], gb[:, 0:N], AF.Silu,
                              scale=sgc[:, s:s + 1], ctr="se")
                BG[s % 2] = ("se", se_silu)
                w("v", "pe", pe_up[s])
                w("v", "se", se_silu)
                w("v", "pe", ST["hq"])
                ve_hmul[j] = op("v", lambda e, o=hq[:, j * N:(j + 1) * N], i0=ub[:, 0:N],
                                sc=suc[:, s:s + 1], i1=hsig[0][:, :]:
                                e.scalar_tensor_tensor(o, i0, sc, i1,
                                                       op0=OP.mult, op1=OP.mult), "ve")
                ST["hsig"][0] = ve_hmul[j]
                BG[2 + s % 2] = ("ve", ve_hmul[j])
            nk = s1 - s0
            for o2 in range(O2T):
                k = o2 * NQV + qv
                if dg_wd[k] == 0:
                    dg_wd[k] = wd_piece(k)
                for dk in (o2 + 1, o2 + 2):
                    kn = dk * NQV + qv
                    if dk < O2T and dg_wd[kn] == 0:
                        dg_wd[kn] = wd_piece(kn)
                db2 = pb[4 + o2 % 2]
                w("t", *BG[4 + o2 % 2])
                w("t", f"wd{wd_slot_of[k]}", dg_wd[k])
                w("t", "ve", ve_hmul[nk - 1])
                slot = wd_slot_of[k]
                for b in range(nk):
                    pe_dn = mm(db2[:, 0:N], wd_s[slot][:, b * 128:(b + 1) * 128],
                               hq[:, b * N:(b + 1) * N], b == 0, b == nk - 1,
                               ctr="pe" if b == nk - 1 else None)
                ST["wd"][slot] = pe_dn
                pe_down_last = pe_dn
                if qv == NQV - 1 and c + 1 < NCH and o2 % 8 == 2:
                    if len(dg_xb) == c + 1 and o2 == 2:
                        dg_xb.append([])
                    w("g", "pe", pe_up[SI - 1])
                    dg_xb[c + 1].append(xb_piece(c + 1, o2 // 8))
                w("v", "pe", pe_dn)
                ve_dev[o2] = op("v", lambda e, o=r[:, o2 * N:(o2 + 1) * N],
                                i0=db2[:, 0:N], sc=sdc[:, o2:o2 + 1]:
                                e.scalar_tensor_tensor(o, i0, sc, o,
                                                       op0=OP.mult, op1=OP.add), "ve")
                BG[4 + o2 % 2] = ("ve", ve_dev[o2])
            if qv + 1 < NQV:
                dg_wd[qv + 1] = wd_piece(qv + 1)
            ST["hq"] = pe_down_last

        if c + 1 < NCH:
            pre = []
            for s in range(3):
                w("g", "pe", ST["woe"][s])
                w("g", f"ws{s % 2}", ST["ws_total"][s % 2])
                pre.append(dma("g", woe_s[s][:, :],
                               bass.AP(woed, s * 128 * HB * 128,
                                       [[HB * 128, 128], [1, HB * 128]]), f"wo{s % 3}"))
            ST["woe_pre"] = pre

        if c + 1 < NCH and DEFER:
            _, _, pe_bc2 = emit_ln(ve_dev, is_ln1=False, defer_apply=True)
            pend = {"pe_bc": pe_bc2, "c": c, "a": 0, "nxt": []}
        elif c + 1 < NCH:
            ve_apply2, _, _ = emit_ln(ve_dev, is_ln1=False)
            nxt = []
            for k in range(4):
                w("y", "ve", ve_apply2[k * 8 + 7])
                vo = dma("y", bass.AP(outd, c * 128 * A * N + k * 8 * N,
                                      [[A * N, 128], [1, 8 * N]]),
                         r[:, k * 8 * N:(k + 1) * 8 * N], "oo")
                w("y", "oo", vo)
                nxt.append(dma("y", r[:, k * 8 * N:(k + 1) * 8 * N],
                               bass.AP(xt, (c + 1) * 128 * A * N + k * 8 * N,
                                       [[A * N, 128], [1, 8 * N]]), f"xr{k}"))
            ds_x.append(nxt)
        else:
            ve_apply2, _, _ = emit_ln(ve_dev, is_ln1=False)
            for k in range(4):
                w("y", "ve", ve_apply2[k * 8 + 7])
                dma("y", bass.AP(outd, c * 128 * A * N + k * 8 * N,
                                 [[A * N, 128], [1, 8 * N]]),
                    r[:, k * 8 * N:(k + 1) * 8 * N], "oo")

    w("y", "oo", C["oo"])
    for e in E:
        for ctr, v in WC[e].items():
            assert v <= C[ctr], (e, ctr, v, C[ctr])

    with es, nc.Block() as block:
        @block.tensor
        def _(eng):
            for f in E["t"]:
                f(eng)

        @block.vector
        def _(eng):
            for f in E["v"]:
                f(eng)

        @block.scalar
        def _(eng):
            for f in E["s"]:
                f(eng)

        @block.sync
        def _(eng):
            for f in E["y"]:
                f(eng)

        @block.gpsimd
        def _(eng):
            for f in E["g"]:
                f(eng)

    return nc


_NC_CACHE = None


def kernel(**inputs):
    global _NC_CACHE
    x = np.asarray(inputs["L0_attn_hidden_states"], dtype=np.float32)
    Wq = np.asarray(inputs["L0_attn_Wq"], dtype=np.float32)
    bq = np.asarray(inputs["L0_attn_bq"], dtype=np.float32)
    Wo = np.asarray(inputs["L0_attn_Wo"], dtype=np.float32)
    Mw = np.asarray(inputs["L0_attn_M"], dtype=np.float32)
    z = np.asarray(inputs["L0_attn_z"], dtype=np.float32)
    wg = np.asarray(inputs["L0_ffn_w_gate"])
    sg = np.asarray(inputs["L0_ffn_s_gate"], dtype=np.float32)
    wu = np.asarray(inputs["L0_ffn_w_up"])
    su = np.asarray(inputs["L0_ffn_s_up"], dtype=np.float32)
    wd = np.asarray(inputs["L0_ffn_w_down"])
    sd = np.asarray(inputs["L0_ffn_s_down"], dtype=np.float32)

    x2 = np.ascontiguousarray(x.reshape(8192, 4096))
    xT = np.ascontiguousarray(x2.reshape(NCORES, NCH, N, A, 128).transpose(0, 1, 4, 3, 2))
    wq_sel = np.ascontiguousarray(Wq.reshape(HB, 512, D)[:, :128, :])
    wq_t = np.ascontiguousarray(wq_sel.reshape(HB, 128, A, 128).transpose(0, 3, 2, 1))
    bq_t = np.ascontiguousarray(bq.reshape(HB, 512)[:, :128].T)
    wo_t = np.ascontiguousarray(Wo.reshape(A, 128, A, 128).transpose(0, 3, 2, 1))
    zt = np.ascontiguousarray(z.T)
    wg_t = np.ascontiguousarray(wg.reshape(SI, 128, A, 128).transpose(0, 3, 2, 1))
    wu_t = np.ascontiguousarray(wu.reshape(SI, 128, A, 128).transpose(0, 3, 2, 1))
    wd_t = np.ascontiguousarray(wd.reshape(O2T, 128, SI, 128).transpose(0, 3, 2, 1))
    sg_t = np.ascontiguousarray(sg.reshape(SI, 128).T)
    su_t = np.ascontiguousarray(su.reshape(SI, 128).T)
    sd_t = np.ascontiguousarray(sd.reshape(O2T, 128).T)

    shared = {
        "wqd": np.ascontiguousarray(wq_t.reshape(HB, 128, A * 128)),
        "bqd": bq_t,
        "wod": np.ascontiguousarray(wo_t.reshape(A, 128, A * 128)),
        "mwd": np.ascontiguousarray(Mw),
        "ztd": zt,
        "wgd": np.ascontiguousarray(wg_t.reshape(SI, 128, A * 128)),
        "wud": np.ascontiguousarray(wu_t.reshape(SI, 128, A * 128)),
        "wdd": np.ascontiguousarray(wd_t.reshape(O2T, 128, SI * 128)),
        "sgd": sg_t, "sud": su_t, "sdd": sd_t,
        "idnd": np.eye(128, dtype=np.float32),
    }
    if _NC_CACHE is None:
        _NC_CACHE = build_nc()
    nc = _NC_CACHE
    in_maps = [dict(shared, xt=np.ascontiguousarray(xT[core].reshape(NCH, 128, A * N)))
               for core in range(NCORES)]
    res = run_bass_kernel_spmd(nc, in_maps, core_ids=list(range(NCORES)))
    outs = np.stack([res.results[i]["outd"] for i in range(NCORES)])
    out = outs.reshape(NCORES, NCH, 128, A, N).transpose(0, 1, 4, 3, 2).reshape(1, 8192, 4096)
    return np.ascontiguousarray(out.astype(np.float32))


# version 33
# speedup vs baseline: 1.0619x; 1.0619x over previous
import contextlib
import numpy as np
import concourse.bass as bass
import concourse.mybir as mybir
from concourse.bass_utils import run_bass_kernel_spmd

F32 = mybir.dt.float32
BF16 = mybir.dt.bfloat16
I8 = mybir.dt.int8
AF = mybir.ActivationFunctionType
OP = mybir.AluOpType

NCORES = 8
D = 4096
A = 32
NCH = 2
N = 512
HB = 8
SI = 86
O2T = 32
QB = [0, 15, 30, 44, 58, 72, 86]
NQV = 6
QMAX = 15
DEFER = True
DRAIN_ALL_AT_0 = False
LN_EPS = 1e-5
ATT_EPS = 1e-6


def build_nc():
    nc = bass.Bass("TRN2", target_bir_lowering=False)

    xt = nc.dram_tensor("xt", [NCH, 128, A * N], F32, kind="ExternalInput")
    wqd = nc.dram_tensor("wqd", [HB, 128, A * 128], F32, kind="ExternalInput")
    bqd = nc.dram_tensor("bqd", [128, HB], F32, kind="ExternalInput")
    wod = nc.dram_tensor("wod", [A, 128, A * 128], F32, kind="ExternalInput")
    mwd = nc.dram_tensor("mwd", [HB, 128, 128], F32, kind="ExternalInput")
    ztd = nc.dram_tensor("ztd", [128, HB], F32, kind="ExternalInput")
    wgd = nc.dram_tensor("wgd", [SI, 128, A * 128], I8, kind="ExternalInput")
    wud = nc.dram_tensor("wud", [SI, 128, A * 128], I8, kind="ExternalInput")
    wdd = nc.dram_tensor("wdd", [O2T, 128, SI * 128], I8, kind="ExternalInput")
    sgd = nc.dram_tensor("sgd", [128, SI], F32, kind="ExternalInput")
    sud = nc.dram_tensor("sud", [128, SI], F32, kind="ExternalInput")
    sdd = nc.dram_tensor("sdd", [128, O2T], F32, kind="ExternalInput")
    idnd = nc.dram_tensor("idnd", [128, 128], F32, kind="ExternalInput")
    outd = nc.dram_tensor("outd", [NCH, 128, A * N], F32, kind="ExternalOutput")
    woed = nc.dram_tensor("woed", [A, 128, HB * 128], BF16)

    es = contextlib.ExitStack()
    sbt = lambda name, shape, dt: es.enter_context(nc.sbuf_tensor(name, shape, dt))
    pst = lambda name: es.enter_context(nc.psum_tensor(name, [128, 512], F32))

    r = sbt("r", [128, A * N], F32)
    xbuf = sbt("xbuf", [128, A * N], BF16)
    hq = sbt("hq", [128, QMAX * N], BF16)
    sigq = [sbt(f"sigq{i}", [128, N], BF16) for i in range(2)]
    retr = sbt("retr", [128, HB * N], BF16)
    wq_s = [sbt(f"wq{i}", [128, 16 * 128], BF16) for i in range(3)]
    woe_s = [sbt(f"woe{i}", [128, HB * 128], BF16) for i in range(3)]
    wg_s = [sbt(f"wgs{i}", [128, 16 * 128], BF16) for i in range(3)]
    wu_s = [sbt(f"wus{i}", [128, 16 * 128], BF16) for i in range(3)]
    wd_s = [sbt(f"wds{i}", [128, QMAX * 128], BF16) for i in range(3)]
    mz = sbt("mz", [128, HB * 128], BF16)
    zrep = sbt("zrep", [128, HB * 128], BF16)
    zb = sbt("zb", [128, HB], BF16)
    bqc = sbt("bqc", [128, HB], F32)
    nbq = sbt("nbq", [128, HB], F32)
    sgc = sbt("sgc", [128, SI], F32)
    suc = sbt("suc", [128, SI], F32)
    sdc = sbt("sdc", [128, O2T], F32)
    scr1 = sbt("scr1", [128, N], F32)
    rden = sbt("rden", [128, N], F32)
    m_sb = scr1[0:1, 0:N]
    hsig = [sbt("hsig0", [128, N], BF16)]
    xsb = [sbt(f"xsb{i}", [128, N], BF16) for i in range(2)]
    sigr = xsb
    sqb = sigq
    ones_row = sbt("ones_row", [1, 128], F32)
    invcol = sbt("invcol", [128, 1], BF16)
    e2_sb = sbt("e2_sb", [1, N], F32)
    isr = sbt("isr", [1, N], F32)
    slb = [sbt(f"slb{i}", [128, 4 * 512], BF16) for i in range(3)]
    idnb = sbt("idnb", [128, 128], BF16)
    woebld = [sbt(f"woebld{i}", [128, HB * 128], BF16) for i in range(2)]
    pb = [pst(f"pb{i}") for i in range(8)]

    semnames = (["pe", "ve", "se", "c0", "c1", "oo"]
                + [f"xr{k}" for k in range(4)] + [f"xb{k}" for k in range(4)]
                + [f"sl{k}" for k in range(3)] + [f"wq{k}" for k in range(3)]
                + [f"wg{k}" for k in range(3)] + [f"wu{k}" for k in range(3)]
                + [f"wd{k}" for k in range(3)] + [f"wo{k}" for k in range(3)]
                + [f"ws{k}" for k in range(2)])
    sems = {k: es.enter_context(nc.semaphore(k)) for k in semnames}

    E = {k: [] for k in ("t", "v", "s", "y", "g")}
    WC = {k: {} for k in E}
    C = {k: 0 for k in semnames}

    def w(e, ctr, v):
        if v is None or v <= 0:
            return
        if WC[e].get(ctr, 0) >= v:
            return
        WC[e][ctr] = v
        E[e].append(lambda eng, s=sems[ctr], vv=v: eng.wait_ge(s, vv))

    def op(e, fn, ctr=None, n=1):
        if ctr is None:
            E[e].append(fn)
            return None
        C[ctr] += n
        E[e].append(lambda eng, f=fn, s=sems[ctr], m=n: f(eng).then_inc(s, m))
        return C[ctr]

    def dma(e, dst, src, key):
        return op(e, lambda eng, d=dst, s=src: eng.dma_start(out=d, in_=s), key, 16)

    def mm(bank_ap, lhs_ap, rhs_ap, start, stop, ctr=None):
        fn = lambda eng, o=bank_ap, l=lhs_ap, rr=rhs_ap, st=start, sp=stop: \
            eng.matmul(o, l, rr, start=st, stop=sp)
        return op("t", fn, ctr)

    def act(out_ap, in_ap, func, bias=0.0, scale=1.0, ctr=None):
        fn = lambda eng, o=out_ap, i=in_ap, f=func, b=bias, sc=scale: \
            eng.activation(o, i, f, bias=b, scale=sc)
        return op("s", fn, ctr)

    BG = [("pe", 0)] * 8

    ST = {
        "wq": [0, 0, 0], "wg": [0, 0, 0], "wu": [0, 0, 0], "wd": [0, 0, 0],
        "woe": [0, 0, 0],
        "sigq": [0, 0],
        "retr": 0,
        "hq": 0,
        "hsig": [0],
        "xs": [0, 0],
        "retr_ve": 0,
        "slb": [0, 0, 0],
    }
    dg_xb = []
    ds_x = []

    def xb_piece(c, k):
        return dma("g", xbuf[:, k * 8 * N:(k + 1) * 8 * N],
                   bass.AP(xt, c * 128 * A * N + k * 8 * N,
                           [[A * N, 128], [1, 8 * N]]), f"xb{k}")

    def slb_dma(h2):
        s, q_ = divmod(h2, 2)
        slot = h2 % 3
        w("g", "pe", ST["slb"][slot])
        return dma("g", slb[slot][:, :],
                   bass.AP(wod, s * 128 * A * 128 + q_ * 2048,
                           [[A * 128, 128], [1, 2048]]), f"sl{slot}")

    dma("y", bqc[:, :], bqd.ap()[:, :], "c0")
    dma("y", sgc[:, :], sgd.ap()[:, :], "c0")
    dma("y", suc[:, :], sud.ap()[:, :], "c0")
    ds_consts = dma("y", sdc[:, :], sdd.ap()[:, :], "c0")
    ds_x.append([dma("y", r[:, k * 8 * N:(k + 1) * 8 * N],
                     bass.AP(xt, k * 8 * N, [[A * N, 128], [1, 8 * N]]), f"xr{k}")
                 for k in range(4)])

    dma("g", bass.AP(mz, 0, [[HB * 128, 128], [128, HB], [1, 128]]),
        bass.AP(mwd, 0, [[128, 128], [128 * 128, HB], [1, 128]]), "c1")
    dma("g", zb[:, :], ztd.ap()[:, :], "c1")
    dg_mz = dma("g", idnb[:, :], idnd.ap()[:, :], "c1")
    dg_xb.append([xb_piece(0, k) for k in range(4)])

    op("v", lambda e: e.memset(ones_row[:, :], 1.0))
    op("v", lambda e: e.memset(invcol[:, :], 1.0 / 4096.0))
    w("v", "c0", ds_consts)
    op("v", lambda e: e.tensor_scalar_mul(nbq[:, :], bqc[:, :], -1.0))
    w("v", "c1", dg_mz)
    ve_consts = None
    for g_ in range(HB):
        src = bass.AP(zb, g_, [[HB, 128], [0, 128]])
        ve_consts = op("v", lambda e, d=zrep[:, g_ * 128:(g_ + 1) * 128], s=src:
                       e.tensor_copy(d, s), "ve" if g_ == HB - 1 else None)

    def emit_ln(r_ready_ve, is_ln1, defer_apply=False):
        pe_stat = [0] * A
        for a in range(A):
            w("s", "ve", r_ready_ve[a])
            w("s", "pe", ST["xs"][a % 2])
            ra = r[:, a * N:(a + 1) * N]
            se_c1 = act(xsb[a % 2][:, :], ra, AF.Copy, ctr="se")
            w("v", "ve", r_ready_ve[a])
            w("v", "pe", ST["xs"][a % 2])
            ve_c1 = op("v", lambda e, o=sqb[a % 2][:, :], i=ra:
                       e.tensor_mul(o, i, i), "ve")
            if a == 0:
                w("t", *BG[6])
                w("t", *BG[7])
                w("t", "ve", ve_consts)
            w("t", "se", se_c1)
            w("t", "ve", ve_c1)
            mm(pb[6][0:1, 0:N], invcol[:, :], xsb[a % 2][:, :], a == 0, a == A - 1)
            pe_stat[a] = mm(pb[7][0:1, 0:N], invcol[:, :], sqb[a % 2][:, :],
                            a == 0, a == A - 1, ctr="pe")
            ST["xs"][a % 2] = pe_stat[a]
        w("s", "pe", pe_stat[A - 1])
        act(m_sb, pb[6][0:1, 0:N], AF.Copy)
        se_rows1 = act(e2_sb[:, :], pb[7][0:1, 0:N], AF.Copy, ctr="se")
        BG[6] = BG[7] = ("se", se_rows1)
        w("v", "se", se_rows1)
        vm = op("v", lambda e: e.tensor_mul(isr[:, :], m_sb, m_sb), "ve")
        w("v", "ve", vm)
        ve_rows = op("v", lambda e: e.scalar_tensor_tensor(
            e2_sb[:, :], e2_sb[:, :], LN_EPS, isr[:, :],
            op0=OP.add, op1=OP.subtract), "ve")
        w("s", "ve", ve_rows)
        seL = act(e2_sb[:, :], e2_sb[:, :], AF.Ln, ctr="se")
        w("s", "se", seL)
        se_rows2 = act(isr[:, :], e2_sb[:, :], AF.Exp, scale=-0.5, ctr="se")
        w("t", *BG[6])
        w("t", *BG[7])
        w("t", "se", se_rows2)
        mm(pb[6][:, 0:N], ones_row[:, :], m_sb, True, True)
        pe_bc = mm(pb[7][:, 0:N], ones_row[:, :], isr[:, :], True, True, ctr="pe")
        if defer_apply:
            return None, None, pe_bc
        ve_apply = [0] * A
        se_cast2 = [0] * A if is_ln1 else None
        w("v", "pe", pe_bc)
        ve_sub = [0] * A

        def esub(a):
            ra = r[:, a * N:(a + 1) * N]
            ve_sub[a] = op("v", lambda e, o=ra, i=pb[6][:, 0:N]:
                           e.tensor_sub(o, o, i), "ve")

        esub(0)
        for a in range(A):
            if a + 1 < A:
                esub(a + 1)
            ra = r[:, a * N:(a + 1) * N]
            w("v", "ve", ve_sub[a])
            ve_apply[a] = op("v", lambda e, o=ra, i=pb[7][:, 0:N]:
                             e.tensor_mul(o, o, i), "ve")
            if is_ln1:
                w("s", "ve", ve_apply[a])
                se_cast2[a] = act(xbuf[:, a * N:(a + 1) * N], ra, AF.Copy, ctr="se")
        BG[6] = BG[7] = ("ve", ve_apply[A - 1])
        return ve_apply, se_cast2, None

    def wq_piece(k):
        hb, half = divmod(k, 2)
        slot = k % 3
        w("g", "pe", ST["wq"][slot])
        return dma("g", wq_s[slot][:, :],
                   bass.AP(wqd, hb * 128 * A * 128 + half * 2048,
                           [[A * 128, 128], [1, 2048]]), f"wq{slot}")

    def wgu_piece(which, k):
        s, half = divmod(k, 2)
        slot = k % 3
        bufs, dr, st_key = ((wg_s, wgd, "wg") if which == "g" else (wu_s, wud, "wu"))
        w("g", "pe", ST[st_key][slot])
        return dma("g", bufs[slot][:, :],
                   bass.AP(dr, s * 128 * A * 128 + half * 2048,
                           [[A * 128, 128], [1, 2048]]), f"{st_key}{slot}")

    wd_slot_of = {}
    wd_seq = [0]

    def wd_piece(k):
        o2, qv = divmod(k, NQV)
        nk = QB[qv + 1] - QB[qv]
        slot = wd_seq[0] % 3
        wd_seq[0] += 1
        wd_slot_of[k] = slot
        w("g", "pe", ST["wd"][slot])
        return dma("g", wd_s[slot][:, 0:nk * 128],
                   bass.AP(wdd, o2 * 128 * SI * 128 + QB[qv] * 128,
                           [[SI * 128, 128], [1, nk * 128]]), f"wd{slot}")

    pend = None

    def drain_pend(upto):
        nonlocal_ = pend
        if nonlocal_ is None:
            return
        cp = nonlocal_["c"]
        w("v", "pe", nonlocal_["pe_bc"])
        last_ve = 0
        vsub = {}

        def esub2(a):
            ra = r[:, a * N:(a + 1) * N]
            vsub[a] = op("v", lambda e, o=ra, i=pb[6][:, 0:N]:
                         e.tensor_sub(o, o, i), "ve")

        lo = nonlocal_["a"]
        esub2(lo)
        for a in range(lo, upto):
            if a + 1 < upto:
                esub2(a + 1)
            ra = r[:, a * N:(a + 1) * N]
            w("v", "ve", vsub[a])
            last_ve = op("v", lambda e, o=ra, i=pb[7][:, 0:N]:
                         e.tensor_mul(o, o, i), "ve")
            if a % 8 == 7:
                k = a // 8
                w("y", "ve", last_ve)
                vo = dma("y", bass.AP(outd, cp * 128 * A * N + k * 8 * N,
                                      [[A * N, 128], [1, 8 * N]]),
                         r[:, k * 8 * N:(k + 1) * 8 * N], "oo")
                w("y", "oo", vo)
                nonlocal_["nxt"].append(
                    dma("y", r[:, k * 8 * N:(k + 1) * 8 * N],
                        bass.AP(xt, (cp + 1) * 128 * A * N + k * 8 * N,
                                [[A * N, 128], [1, 8 * N]]), f"xr{k}"))
        nonlocal_["a"] = upto
        if upto == A:
            BG[6] = BG[7] = ("ve", last_ve)
            ds_x.append(nonlocal_["nxt"])

    for c in range(NCH):
        dg_wq = [0] * (2 * HB)
        for k in range(3):
            dg_wq[k] = wq_piece(k)
        pe_q = [0] * HB
        pe_nd = [0] * HB
        ve_retr = [0] * HB

        def emit_q(hb):
            qb_ = pb[hb % 2]
            w("t", *BG[hb % 2])
            for a in range(A):
                if a % 8 == 0:
                    w("t", f"xb{a // 8}", dg_xb[c][a // 8])
                if a == 0:
                    w("t", f"wq{(hb * 2) % 3}", dg_wq[hb * 2])
                if a == 16:
                    w("t", f"wq{(hb * 2 + 1) % 3}", dg_wq[hb * 2 + 1])
                piece = wq_s[(hb * 2 + a // 16) % 3]
                pe_q[hb] = mm(qb_[:, 0:N], piece[:, (a % 16) * 128:(a % 16 + 1) * 128],
                              xbuf[:, a * N:(a + 1) * N], a == 0, a == A - 1,
                              ctr="pe" if a == A - 1 else None)
            ST["wq"][(hb * 2) % 3] = pe_q[hb]
            ST["wq"][(hb * 2 + 1) % 3] = pe_q[hb]
            for k in (hb * 2 + 3, hb * 2 + 4):
                if k < 2 * HB and k >= 3:
                    dg_wq[k] = wq_piece(k)

        prev_se2 = 0
        emit_q(0)
        for hb in range(HB):
            qb_ = pb[hb % 2]
            w("s", "pe", pe_q[hb])
            w("s", "ve", ve_consts)
            w("s", "pe", ST["sigq"][hb % 2])
            w("s", "se", prev_se2)
            se1 = act(scr1[:, :], qb_[:, 0:N], AF.Relu, bias=nbq[:, hb:hb + 1],
                      scale=-1.0, ctr="se")
            w("s", "se", se1)
            se2 = act(sigq[hb % 2][:, :], scr1[:, :], AF.Exp, scale=-1.0, ctr="se")
            prev_se2 = se2
            se_elu = act(sigr[hb % 2][:, :], qb_[:, 0:N], AF.Relu,
                         bias=bqc[:, hb:hb + 1], ctr="se")
            BG[hb % 2] = ("se", se_elu)
            if pend is not None:
                drain_pend(A if DRAIN_ALL_AT_0 else 4 * (hb + 1))
                if hb == HB - 1 or DRAIN_ALL_AT_0:
                    pend = None
            if hb + 1 < HB:
                emit_q(hb + 1)
            nb_, db_ = pb[2 + hb % 2], pb[4 + hb % 2]
            w("t", *BG[2 + hb % 2])
            w("t", *BG[4 + hb % 2])
            w("t", "se", se_elu)
            w("t", "c1", dg_mz)
            mm(nb_[:, 0:N], mz[:, hb * 128:(hb + 1) * 128], sigq[hb % 2][:, :],
               True, False)
            mm(nb_[:, 0:N], mz[:, hb * 128:(hb + 1) * 128], sigr[hb % 2][:, :],
               False, True)
            mm(db_[:, 0:N], zrep[:, hb * 128:(hb + 1) * 128], sigq[hb % 2][:, :],
               True, False)
            pe_nd[hb] = mm(db_[:, 0:N], zrep[:, hb * 128:(hb + 1) * 128],
                           sigr[hb % 2][:, :], False, True, ctr="pe")
            ST["sigq"][hb % 2] = pe_nd[hb]
            w("v", "pe", pe_nd[hb])
            w("v", "pe", ST["retr"])
            w("v", "ve", ve_retr[hb - 1] if hb > 0 else ST["retr_ve"])
            v1 = op("v", lambda e, o=rden[:, :], i=db_[:, 0:N]:
                    e.tensor_scalar_add(o, i, ATT_EPS), "ve")
            w("v", "ve", v1)
            v2 = op("v", lambda e, o=rden[:, :]: e.reciprocal(o, o), "ve")
            w("v", "ve", v2)
            ve_retr[hb] = op("v", lambda e, o=retr[:, hb * N:(hb + 1) * N],
                             i0=nb_[:, 0:N], i1=rden[:, :]: e.tensor_mul(o, i0, i1), "ve")
            BG[2 + hb % 2] = BG[4 + hb % 2] = ("ve", ve_retr[hb])
        ST["retr_ve"] = ve_retr[HB - 1]

        dg_wg = [0] * (2 * SI)
        dg_wu = [0] * (2 * SI)
        for k in range(3):
            dg_wg[k] = wgu_piece("g", k)
            dg_wu[k] = wgu_piece("u", k)

        pe_ao = [0] * A
        ve_aoev = [0] * A
        if c == 0:
            dg_slb = [0] * (2 * A)
            for k in range(3):
                w("g", "pe", pe_q[k])
                dg_slb[k] = slb_dma(k)
            bld_ve = [0] * A
            ds_store = [0] * A
            for s in range(A):
                ve3 = 0
                for q_ in (0, 1):
                    h2 = s * 2 + q_
                    bank = pb[6 + (h2 % 2)]
                    w("t", *BG[6 + (h2 % 2)])
                    w("t", f"sl{h2 % 3}", dg_slb[h2])
                    w("t", "c1", dg_mz)
                    pe_bld = 0
                    for r_ in range(4):
                        rhs = bass.AP(slb[h2 % 3], r_ * 128,
                                      [[2048, 128], [512, 4], [1, 128]])
                        pe_bld = mm(bank[:, 0:512], idnb[:, :], rhs,
                                    r_ == 0, r_ == 3, ctr="pe" if r_ == 3 else None)
                    ST["slb"][h2 % 3] = pe_bld
                    if h2 + 3 < 2 * A:
                        dg_slb[h2 + 3] = slb_dma(h2 + 3)
                    w("v", "pe", pe_bld)
                    if s >= 2 and q_ == 0:
                        w("v", "pe", pe_ao[s - 2])
                        w("v", f"ws{s % 2}", ds_store[s - 2])
                    ve3 = op("v", lambda e,
                             o=woebld[s % 2][:, q_ * 512:(q_ + 1) * 512],
                             i=bank[:, 0:512]: e.tensor_copy(o, i), "ve")
                    BG[6 + (h2 % 2)] = ("ve", ve3)
                bld_ve[s] = ve3
                w("y", "ve", bld_ve[s])
                ds_store[s] = dma("y", bass.AP(woed, s * 128 * HB * 128,
                                               [[HB * 128, 128], [1, HB * 128]]),
                                  woebld[s % 2][:, :], f"ws{s % 2}")
                ab = pb[s % 2]
                w("t", *BG[s % 2])
                w("t", "ve", ve_retr[HB - 1])
                w("t", "ve", bld_ve[s])
                for g_ in range(HB):
                    pe_ao[s] = mm(ab[:, 0:N], woebld[s % 2][:, g_ * 128:(g_ + 1) * 128],
                                  retr[:, g_ * N:(g_ + 1) * N], g_ == 0, g_ == HB - 1,
                                  ctr="pe" if g_ == HB - 1 else None)
                w("v", "pe", pe_ao[s])
                w("v", f"xr{s // 8}", ds_x[c][s // 8])
                ve_aoev[s] = op("v", lambda e, o=r[:, s * N:(s + 1) * N], i=ab[:, 0:N]:
                                e.tensor_add(o, o, i), "ve")
                BG[s % 2] = ("ve", ve_aoev[s])
        else:
            ds_woe = [0] * A
            for s in range(3):
                ds_woe[s] = ST["woe_pre"][s]
            for s in range(A):
                ab = pb[s % 2]
                w("t", *BG[s % 2])
                w("t", "ve", ve_retr[HB - 1])
                w("t", f"wo{s % 3}", ds_woe[s])
                for g_ in range(HB):
                    pe_ao[s] = mm(ab[:, 0:N], woe_s[s % 3][:, g_ * 128:(g_ + 1) * 128],
                                  retr[:, g_ * N:(g_ + 1) * N], g_ == 0, g_ == HB - 1,
                                  ctr="pe" if g_ == HB - 1 else None)
                w("v", "pe", pe_ao[s])
                w("v", f"xr{s // 8}", ds_x[c][s // 8])
                ve_aoev[s] = op("v", lambda e, o=r[:, s * N:(s + 1) * N], i=ab[:, 0:N]:
                                e.tensor_add(o, o, i), "ve")
                BG[s % 2] = ("ve", ve_aoev[s])
                ST["woe"][s % 3] = pe_ao[s]
                if s + 3 < A:
                    w("y", "pe", pe_ao[s])
                    w("y", f"ws{(s + 3) % 2}", ST["ws_total"][(s + 3) % 2])
                    ds_woe[s + 3] = dma("y", woe_s[s % 3][:, :],
                                        bass.AP(woed, (s + 3) * 128 * HB * 128,
                                                [[HB * 128, 128], [1, HB * 128]]), f"wo{s % 3}")
        if c == 0:
            ST["ws_store"] = list(ds_store)
            ST["ws_total"] = {0: C["ws0"], 1: C["ws1"]}
        ST["retr"] = pe_ao[A - 1]

        ve_apply1, se_cast2, _ = emit_ln(ve_aoev, is_ln1=True)

        pe_gate = [0] * SI
        pe_up = [0] * SI
        dg_wd = [0] * (NQV * O2T)
        pe_down_last = 0
        ve_dev = [0] * A
        for qv in range(NQV):
            s0, s1 = QB[qv], QB[qv + 1]
            ve_hmul = [0] * (s1 - s0)
            for s in range(s0, s1):
                j = s - s0
                gb, ub = pb[s % 2], pb[2 + s % 2]
                w("t", *BG[s % 2])
                for a in range(A):
                    if a == 0:
                        w("t", f"wg{(s * 2) % 3}", dg_wg[s * 2])
                    if a == 16:
                        w("t", f"wg{(s * 2 + 1) % 3}", dg_wg[s * 2 + 1])
                    if qv == 0 and s == 0:
                        w("t", "se", se_cast2[a])
                    piece = wg_s[(s * 2 + a // 16) % 3]
                    pe_gate[s] = mm(gb[:, 0:N], piece[:, (a % 16) * 128:(a % 16 + 1) * 128],
                                    xbuf[:, a * N:(a + 1) * N], a == 0, a == A - 1,
                                    ctr="pe" if a == A - 1 else None)
                ST["wg"][(s * 2) % 3] = pe_gate[s]
                ST["wg"][(s * 2 + 1) % 3] = pe_gate[s]
                w("t", *BG[2 + s % 2])
                for a in range(A):
                    if a == 0:
                        w("t", f"wu{(s * 2) % 3}", dg_wu[s * 2])
                    if a == 16:
                        w("t", f"wu{(s * 2 + 1) % 3}", dg_wu[s * 2 + 1])
                    piece = wu_s[(s * 2 + a // 16) % 3]
                    pe_up[s] = mm(ub[:, 0:N], piece[:, (a % 16) * 128:(a % 16 + 1) * 128],
                                  xbuf[:, a * N:(a + 1) * N], a == 0, a == A - 1,
                                  ctr="pe" if a == A - 1 else None)
                ST["wu"][(s * 2) % 3] = pe_up[s]
                ST["wu"][(s * 2 + 1) % 3] = pe_up[s]
                for k in (s * 2 + 3, s * 2 + 4):
                    if 3 <= k < 2 * SI:
                        dg_wg[k] = wgu_piece("g", k)
                        dg_wu[k] = wgu_piece("u", k)
                if s == s1 - 2:
                    for kk in (qv, NQV + qv, 2 * NQV + qv):
                        if dg_wd[kk] == 0:
                            dg_wd[kk] = wd_piece(kk)
                w("s", "pe", pe_gate[s])
                w("s", "ve", ST["hsig"][0])
                w("s", "c0", ds_consts)
                se_silu = act(hsig[0][:, :], gb[:, 0:N], AF.Silu,
                              scale=sgc[:, s:s + 1], ctr="se")
                BG[s % 2] = ("se", se_silu)
                w("v", "pe", pe_up[s])
                w("v", "se", se_silu)
                w("v", "pe", ST["hq"])
                ve_hmul[j] = op("v", lambda e, o=hq[:, j * N:(j + 1) * N], i0=ub[:, 0:N],
                                sc=suc[:, s:s + 1], i1=hsig[0][:, :]:
                                e.scalar_tensor_tensor(o, i0, sc, i1,
                                                       op0=OP.mult, op1=OP.mult), "ve")
                ST["hsig"][0] = ve_hmul[j]
                BG[2 + s % 2] = ("ve", ve_hmul[j])
            nk = s1 - s0
            for o2 in range(O2T):
                k = o2 * NQV + qv
                if dg_wd[k] == 0:
                    dg_wd[k] = wd_piece(k)
                for dk in (o2 + 1, o2 + 2):
                    kn = dk * NQV + qv
                    if dk < O2T and dg_wd[kn] == 0:
                        dg_wd[kn] = wd_piece(kn)
                db2 = pb[4 + o2 % 2]
                w("t", *BG[4 + o2 % 2])
                w("t", f"wd{wd_slot_of[k]}", dg_wd[k])
                w("t", "ve", ve_hmul[nk - 1])
                slot = wd_slot_of[k]
                for b in range(nk):
                    pe_dn = mm(db2[:, 0:N], wd_s[slot][:, b * 128:(b + 1) * 128],
                               hq[:, b * N:(b + 1) * N], b == 0, b == nk - 1,
                               ctr="pe" if b == nk - 1 else None)
                ST["wd"][slot] = pe_dn
                pe_down_last = pe_dn
                if qv == NQV - 1 and c + 1 < NCH and o2 % 8 == 2:
                    if len(dg_xb) == c + 1 and o2 == 2:
                        dg_xb.append([])
                    w("g", "pe", pe_up[SI - 1])
                    dg_xb[c + 1].append(xb_piece(c + 1, o2 // 8))
                w("v", "pe", pe_dn)
                ve_dev[o2] = op("v", lambda e, o=r[:, o2 * N:(o2 + 1) * N],
                                i0=db2[:, 0:N], sc=sdc[:, o2:o2 + 1]:
                                e.scalar_tensor_tensor(o, i0, sc, o,
                                                       op0=OP.mult, op1=OP.add), "ve")
                BG[4 + o2 % 2] = ("ve", ve_dev[o2])
            if qv + 1 < NQV:
                dg_wd[qv + 1] = wd_piece(qv + 1)
            ST["hq"] = pe_down_last

        if c + 1 < NCH:
            pre = []
            for s in range(3):
                w("y", "pe", ST["woe"][s])
                w("y", f"ws{s % 2}", ST["ws_total"][s % 2])
                pre.append(dma("y", woe_s[s][:, :],
                               bass.AP(woed, s * 128 * HB * 128,
                                       [[HB * 128, 128], [1, HB * 128]]), f"wo{s % 3}"))
            ST["woe_pre"] = pre

        if c + 1 < NCH and DEFER:
            _, _, pe_bc2 = emit_ln(ve_dev, is_ln1=False, defer_apply=True)
            pend = {"pe_bc": pe_bc2, "c": c, "a": 0, "nxt": []}
        elif c + 1 < NCH:
            ve_apply2, _, _ = emit_ln(ve_dev, is_ln1=False)
            nxt = []
            for k in range(4):
                w("y", "ve", ve_apply2[k * 8 + 7])
                vo = dma("y", bass.AP(outd, c * 128 * A * N + k * 8 * N,
                                      [[A * N, 128], [1, 8 * N]]),
                         r[:, k * 8 * N:(k + 1) * 8 * N], "oo")
                w("y", "oo", vo)
                nxt.append(dma("y", r[:, k * 8 * N:(k + 1) * 8 * N],
                               bass.AP(xt, (c + 1) * 128 * A * N + k * 8 * N,
                                       [[A * N, 128], [1, 8 * N]]), f"xr{k}"))
            ds_x.append(nxt)
        else:
            ve_apply2, _, _ = emit_ln(ve_dev, is_ln1=False)
            for k in range(4):
                w("y", "ve", ve_apply2[k * 8 + 7])
                dma("y", bass.AP(outd, c * 128 * A * N + k * 8 * N,
                                 [[A * N, 128], [1, 8 * N]]),
                    r[:, k * 8 * N:(k + 1) * 8 * N], "oo")

    w("y", "oo", C["oo"])
    for e in E:
        for ctr, v in WC[e].items():
            assert v <= C[ctr], (e, ctr, v, C[ctr])

    with es, nc.Block() as block:
        @block.tensor
        def _(eng):
            for f in E["t"]:
                f(eng)

        @block.vector
        def _(eng):
            for f in E["v"]:
                f(eng)

        @block.scalar
        def _(eng):
            for f in E["s"]:
                f(eng)

        @block.sync
        def _(eng):
            for f in E["y"]:
                f(eng)

        @block.gpsimd
        def _(eng):
            for f in E["g"]:
                f(eng)

    return nc


_NC_CACHE = None


def kernel(**inputs):
    global _NC_CACHE
    x = np.asarray(inputs["L0_attn_hidden_states"], dtype=np.float32)
    Wq = np.asarray(inputs["L0_attn_Wq"], dtype=np.float32)
    bq = np.asarray(inputs["L0_attn_bq"], dtype=np.float32)
    Wo = np.asarray(inputs["L0_attn_Wo"], dtype=np.float32)
    Mw = np.asarray(inputs["L0_attn_M"], dtype=np.float32)
    z = np.asarray(inputs["L0_attn_z"], dtype=np.float32)
    wg = np.asarray(inputs["L0_ffn_w_gate"])
    sg = np.asarray(inputs["L0_ffn_s_gate"], dtype=np.float32)
    wu = np.asarray(inputs["L0_ffn_w_up"])
    su = np.asarray(inputs["L0_ffn_s_up"], dtype=np.float32)
    wd = np.asarray(inputs["L0_ffn_w_down"])
    sd = np.asarray(inputs["L0_ffn_s_down"], dtype=np.float32)

    x2 = np.ascontiguousarray(x.reshape(8192, 4096))
    xT = np.ascontiguousarray(x2.reshape(NCORES, NCH, N, A, 128).transpose(0, 1, 4, 3, 2))
    wq_sel = np.ascontiguousarray(Wq.reshape(HB, 512, D)[:, :128, :])
    wq_t = np.ascontiguousarray(wq_sel.reshape(HB, 128, A, 128).transpose(0, 3, 2, 1))
    bq_t = np.ascontiguousarray(bq.reshape(HB, 512)[:, :128].T)
    wo_t = np.ascontiguousarray(Wo.reshape(A, 128, A, 128).transpose(0, 3, 2, 1))
    zt = np.ascontiguousarray(z.T)
    wg_t = np.ascontiguousarray(wg.reshape(SI, 128, A, 128).transpose(0, 3, 2, 1))
    wu_t = np.ascontiguousarray(wu.reshape(SI, 128, A, 128).transpose(0, 3, 2, 1))
    wd_t = np.ascontiguousarray(wd.reshape(O2T, 128, SI, 128).transpose(0, 3, 2, 1))
    sg_t = np.ascontiguousarray(sg.reshape(SI, 128).T)
    su_t = np.ascontiguousarray(su.reshape(SI, 128).T)
    sd_t = np.ascontiguousarray(sd.reshape(O2T, 128).T)

    shared = {
        "wqd": np.ascontiguousarray(wq_t.reshape(HB, 128, A * 128)),
        "bqd": bq_t,
        "wod": np.ascontiguousarray(wo_t.reshape(A, 128, A * 128)),
        "mwd": np.ascontiguousarray(Mw),
        "ztd": zt,
        "wgd": np.ascontiguousarray(wg_t.reshape(SI, 128, A * 128)),
        "wud": np.ascontiguousarray(wu_t.reshape(SI, 128, A * 128)),
        "wdd": np.ascontiguousarray(wd_t.reshape(O2T, 128, SI * 128)),
        "sgd": sg_t, "sud": su_t, "sdd": sd_t,
        "idnd": np.eye(128, dtype=np.float32),
    }
    if _NC_CACHE is None:
        _NC_CACHE = build_nc()
    nc = _NC_CACHE
    in_maps = [dict(shared, xt=np.ascontiguousarray(xT[core].reshape(NCH, 128, A * N)))
               for core in range(NCORES)]
    res = run_bass_kernel_spmd(nc, in_maps, core_ids=list(range(NCORES)))
    outs = np.stack([res.results[i]["outd"] for i in range(NCORES)])
    out = outs.reshape(NCORES, NCH, 128, A, N).transpose(0, 1, 4, 3, 2).reshape(1, 8192, 4096)
    return np.ascontiguousarray(out.astype(np.float32))
